# version 02 (final)
# baseline (speedup 1.0000x reference)
import numpy as np
from contextlib import ExitStack

import concourse.bass as bass
import concourse.mybir as mybir
from concourse.bass_utils import run_bass_kernel_spmd

F32 = mybir.dt.float32
BF16 = mybir.dt.bfloat16
AF = mybir.ActivationFunctionType
ALU = mybir.AluOpType

NCORE = 8
B_, L_, D_ = 2, 4096, 2048
NTOK = B_ * L_
TPC = NTOK // NCORE
HALO = 3
C64 = 64
FFH = 8192
EPS0 = 1e-5
EPS1 = 1e-6


class Bld:
    def __init__(self, tag):
        self.nc = bass.Bass("TRN2", target_bir_lowering=False)
        self.tag = tag
        self.prog = []
        self.stack = ExitStack()
        self.sem = self.stack.enter_context(self.nc.semaphore("gsem"))
        self._nm = 0
        self.ps = [self.stack.enter_context(
            self.nc.psum_tensor(f"psb{i}", [128, 512], F32)) for i in range(8)]

    def inp(self, name, shape, dt=F32):
        return self.nc.dram_tensor(name, list(shape), dt,
                                   kind="ExternalInput").ap()

    def outp(self, name, shape):
        return self.nc.dram_tensor(name, list(shape), F32, kind="ExternalOutput").ap()

    def sb(self, p, f, dt=F32):
        self._nm += 1
        return self.stack.enter_context(
            self.nc.sbuf_tensor(f"s{self._nm}", [p, f], dt))

    @staticmethod
    def _nm_of(ap):
        try:
            return ap.tensor.name
        except Exception:
            return None

    def dma(self, out, in_, q='sync'):
        meta = (self._nm_of(out), self._nm_of(in_))
        self.prog.append((q, lambda e, o=out, i=in_: e.dma_start(out=o, in_=i),
                          16, meta))

    def op(self, eng, fn):
        self.prog.append((eng, fn, 1, None))

    def mm(self, out, lhsT, rhs, start=True, stop=True):
        self.op('tensor', lambda e, o=out, l=lhsT, r=rhs, s1=start, s2=stop:
                e.matmul(o, l, r, start=s1, stop=s2))

    def trans(self, out, in_, eye):
        self.op('tensor', lambda e, o=out, i=in_, y=eye: e.transpose(o, i, y))

    def act(self, out, in_, func, bias=None, scale=None):
        kw = {}
        if bias is not None:
            kw['bias'] = bias
        if scale is not None:
            kw['scale'] = scale
        self.op('scalar', lambda e, o=out, i=in_, f=func, k=kw:
                e.activation(o, i, f, **k))

    def tt(self, out, a, b, opk, eng='vector'):
        self.op(eng, lambda e, o=out, x=a, y=b, op=opk: e.tensor_tensor(o, x, y, op))

    def ts(self, out, in0, s1, op0, s2=None, op1=None):
        if s2 is None:
            self.op('vector', lambda e, o=out, i=in0, a=s1, p=op0:
                    e.tensor_scalar(o, i, a, None, op0=p))
        else:
            self.op('vector', lambda e, o=out, i=in0, a=s1, b=s2, p=op0, q=op1:
                    e.tensor_scalar(o, i, a, b, op0=p, op1=q))

    def stt(self, out, in0, scalar, in1, op0, op1):
        self.op('vector', lambda e, o=out, i=in0, s=scalar, j=in1, p=op0, q=op1:
                e.scalar_tensor_tensor(o, i, s, j, p, q))

    def copy(self, out, in_, eng='vector'):
        self.op(eng, lambda e, o=out, i=in_: e.tensor_copy(o, i))

    def memset(self, ap, v):
        self.op('gpsimd', lambda e, a=ap, x=v: e.memset(a, x))

    def cbias(self, v, p=128):
        if not hasattr(self, '_cbd'):
            self._cbd = {}
        if v not in self._cbd:
            t = self.sb(128, 1)
            self.memset(t[:, :], float(v))
            self._cbd[v] = t
        return self._cbd[v][0:p, 0:1]

    def sel(self, out, mask, on_true, on_false):
        self.op('vector', lambda e, o=out, m=mask, a=on_true, b2=on_false:
                e.select(o, m, a, b2))

    def scan(self, out, d0, d1, init, op0, op1):
        self.op('vector', lambda e, o=out, a=d0, b=d1, i=init, p=op0, q=op1:
                e.tensor_tensor_scan(o, a, b, i, op0=p, op1=q))

    def finalize(self):
        waits = []
        cum = 0
        for item in self.prog:
            waits.append(cum)
            cum += item[2]
        total = cum
        RELAX = {'tensor', 'sync'}
        with self.nc.Block() as block:
            def emit_for(eng_name):
                def run(e):
                    last_target = 0
                    own_since = 0
                    run_bufs = set()
                    for item, wv in zip(self.prog, waits):
                        en, fn, inc, meta = item
                        if en != eng_name:
                            continue
                        conflict = False
                        if meta is not None:
                            onm, inm = meta
                            if (onm is not None and onm in run_bufs) or \
                               (inm is not None and inm in run_bufs):
                                conflict = True
                        need = wv > last_target + own_since
                        if eng_name not in RELAX:
                            need = need or wv > 0
                        if need or conflict:
                            if wv > 0:
                                e.wait_ge(self.sem, wv)
                            last_target = wv
                            own_since = 0
                            run_bufs = set()
                        fn(e).then_inc(self.sem, inc)
                        own_since += inc
                        if meta is not None:
                            onm, inm = meta
                            if onm is not None:
                                run_bufs.add(onm)
                            if inm is not None:
                                run_bufs.add(inm)
                    if eng_name == 'sync':
                        e.wait_ge(self.sem, total)
                return run
            block.sync(emit_for('sync'))
            block.scalar(emit_for('scalar'))
            block.vector(emit_for('vector'))
            block.tensor(emit_for('tensor'))
            block.gpsimd(emit_for('gpsimd'))
        self.stack.close()
        return self.nc


def _run(nc, in_maps):
    res = run_bass_kernel_spmd(nc, in_maps, core_ids=list(range(NCORE)))
    return res.results


def part_reduce(b, psum, tiles, n, sq_tmp=None, start=True, stop=True):
    k = len(tiles)
    for i, t in enumerate(tiles):
        src = t
        if sq_tmp is not None:
            b.act(sq_tmp[:, :n], t, AF.Square)
            src = sq_tmp[:, :n]
        b.mm(psum[:, :n], b._ones[:, :], src,
             start=(start and i == 0), stop=(stop and i == k - 1))


def layernorm_f(b, out_tiles, in_tiles, n, scale_tiles=None, bias_tiles=None,
                eps=EPS0, d=2048):
    ps_m = b.ps[6]
    ps_s = b.ps[7]
    sq = b._scr0
    part_reduce(b, ps_m, in_tiles, n)
    part_reduce(b, ps_s, in_tiles, n, sq_tmp=sq)
    mu = b._scr1
    rs = b._scr2
    b.ts(mu[:, :n], ps_m[:, :n], 1.0 / d, ALU.mult)
    b.act(rs[:, :n], mu[:, :n], AF.Square)
    b.stt(rs[:, :n], ps_s[:, :n], 1.0 / d, rs[:, :n], ALU.mult, ALU.subtract)
    b.act(rs[:, :n], rs[:, :n], AF.Sqrt, bias=b.cbias(eps))
    b.op('vector', lambda e, o=rs, n_=n: e.reciprocal(o[:, :n_], o[:, :n_]))
    for o, i in zip(out_tiles, in_tiles):
        b.tt(o, i, mu[:, :n], ALU.subtract)
        b.tt(o, o, rs[:, :n], ALU.mult)
    if scale_tiles is not None:
        for o, s in zip(out_tiles, scale_tiles):
            b.ts(o, o, s, ALU.mult)
    if bias_tiles is not None:
        for o, s in zip(out_tiles, bias_tiles):
            b.ts(o, o, s, ALU.add)


def proj_f(b, out_ps, wT_tiles, x_tiles, n, start=True, stop=True):
    kn = len(wT_tiles)
    for k in range(kn):
        b.mm(out_ps[:, :n], wT_tiles[k], x_tiles[k],
             start=(start and k == 0), stop=(stop and k == kn - 1))


def build_LA():
    b = Bld("LA")
    nc = b.nc
    xT = b.inp("xT", [D_, TPC])
    WqT = b.inp("WqT", [D_, 128])
    WkT = b.inp("WkT", [D_, 128])
    W1T = b.inp("W1T", [D_, FFH])
    W3T = b.inp("W3T", [D_, FFH])
    W2T = b.inp("W2T", [FFH, D_])
    H128 = b.inp("H128", [128, 128])
    o_xn = b.outp("xnT", [D_, TPC])
    o_q = b.outp("qT", [128, TPC])
    o_k = b.outp("kT", [128, TPC])
    o_gf = b.outp("gfT", [128, TPC])
    o_g = b.outp("gT", [D_, TPC])

    xn = [b.sb(128, 512) for _ in range(16)]
    b._ones = b.sb(128, 128)
    b._scr0 = b.sb(128, 512)
    b._scr1 = b.sb(128, 512)
    b._scr2 = b.sb(128, 512)
    h128 = b.sb(128, 128)
    qs = b.sb(128, 512)
    ks = b.sb(128, 512)
    hid = [b.sb(128, 512) for _ in range(64)]
    wblk1 = [b.sb(128, 128) for _ in range(16)]
    wblk3 = [b.sb(128, 128) for _ in range(16)]
    tmp = b.sb(128, 512)
    gout = [b.sb(128, 512)]

    b.memset(b._ones[:, :], 1.0)
    b.dma(h128[:, :], H128[:, :])

    for sl in range(2):
        t0 = 512 * sl
        n = 512
        for k in range(16):
            b.dma(xn[k][:, :n], xT[128 * k:128 * (k + 1), t0:t0 + n])
        layernorm_f(b, [xn[k][:, :n] for k in range(16)],
                    [xn[k][:, :n] for k in range(16)], n)
        for k in range(16):
            b.dma(o_xn[128 * k:128 * (k + 1), t0:t0 + n], xn[k][:, :n])
        for k in range(16):
            b.dma(wblk1[k][:, :], WqT[128 * k:128 * (k + 1), :])
            b.dma(wblk3[k][:, :], WkT[128 * k:128 * (k + 1), :])
        proj_f(b, b.ps[0], [wblk1[k][:, :] for k in range(16)],
               [xn[k][:, :n] for k in range(16)], n)
        b.act(qs[:, :n], b.ps[0][:, :n], AF.Exp)
        part_reduce(b, b.ps[1], [qs[:, :n]], n)
        b.op('vector', lambda e, n_=n: e.reciprocal(b._scr0[:, :n_], b.ps[1][:, :n_]))
        b.tt(qs[:, :n], qs[:, :n], b._scr0[:, :n], ALU.mult)
        b.dma(o_q[:, t0:t0 + n], qs[:, :n])
        proj_f(b, b.ps[0], [wblk3[k][:, :] for k in range(16)],
               [xn[k][:, :n] for k in range(16)], n)
        b.act(ks[:, :n], b.ps[0][:, :n], AF.Sigmoid)
        b.dma(o_k[:, t0:t0 + n], ks[:, :n])
        b.act(b._scr0[:, :n], ks[:, :n], AF.Exp)
        b.act(b._scr0[:, :n], b._scr0[:, :n], AF.Ln, bias=b.cbias(1.0))
        b.ts(b._scr0[:, :n], b._scr0[:, :n], -1.0, ALU.mult)
        b.dma(o_gf[:, t0:t0 + n], b._scr0[:, :n])

        m = 512
        for hg in range(16):
            for j in range(4):
                ht = 4 * hg + j
                for k in range(16):
                    b.dma(wblk1[k][:, :128],
                          W1T[128 * k:128 * (k + 1), 128 * ht:128 * (ht + 1)])
                    b.dma(wblk3[k][:, :128],
                          W3T[128 * k:128 * (k + 1), 128 * ht:128 * (ht + 1)])
                proj_f(b, b.ps[0],
                       [wblk1[k][:, :128] for k in range(16)],
                       [xn[k][:, :m] for k in range(16)], m)
                proj_f(b, b.ps[1],
                       [wblk3[k][:, :128] for k in range(16)],
                       [xn[k][:, :m] for k in range(16)], m)
                b.act(hid[ht][:, :m], b.ps[0][:, :m], AF.Silu)
                b.tt(hid[ht][:, :m], hid[ht][:, :m], b.ps[1][:, :m], ALU.mult)
        sc = 1.0 / np.sqrt(8192.0)
        for j4 in range(16):
            for j in range(4):
                b.mm(b.ps[j][:, :m], h128[:, :], hid[4 * j4 + j][:, :m])
            for j in range(4):
                b.act(hid[4 * j4 + j][:, :m], b.ps[j][:, :m], AF.Copy,
                      scale=float(sc))
        for st in range(6):
            h = 1 << st
            for base in range(0, 64, 2 * h):
                for r in range(h):
                    a_ = base + r
                    c_ = base + r + h
                    b.tt(tmp[:, :m], hid[a_][:, :m], hid[c_][:, :m], ALU.add)
                    b.tt(hid[c_][:, :m], hid[a_][:, :m], hid[c_][:, :m],
                         ALU.subtract)
                    b.copy(hid[a_][:, :m], tmp[:, :m])
        for ct in range(16):
            for J in range(64):
                if J % 16 == 0:
                    for kk in range(16):
                        b.dma(wblk1[kk][:, :128],
                              W2T[128 * (J + kk):128 * (J + kk + 1),
                                  128 * ct:128 * (ct + 1)])
                b.mm(b.ps[ct % 2][:, :m],
                     wblk1[J % 16][:, :128], hid[J][:, :m],
                     start=(J == 0), stop=(J == 63))
            b.act(gout[0][:, :m], b.ps[ct % 2][:, :m], AF.Silu)
            b.dma(o_g[128 * ct:128 * (ct + 1), t0:t0 + m], gout[0][:, :m])
    return b.finalize()


def build_LB():
    b = Bld("LB")
    DV = 256
    qT = b.inp("qT", [128, NTOK])
    kT = b.inp("kT", [128, NTOK])
    gfT = b.inp("gfT", [128, NTOK])
    vtok = b.inp("vtok", [NTOK, DV])
    EYE = b.inp("EYE", [128, 128])
    TRIU = b.inp("TRIU", [64, 64], mybir.dt.uint8)
    o_o = b.outp("o_o", [NTOK, DV])

    q = b.sb(128, L_)
    k_ = b.sb(128, L_)
    gf = b.sb(128, L_)
    bt = b.sb(128, L_)
    eb = b.sb(128, L_)
    rmask = b.sb(128, L_)
    eye = b.sb(128, 128)
    triu = b.sb(64, 64, mybir.dt.uint8)
    S = b.sb(128, DV)
    vch = b.sb(64, DV)
    at = b.sb(64, 64)
    kd = b.sb(128, 64)
    kdt = b.sb(64, 128)
    e1 = b.sb(128, 64)
    e2 = b.sb(128, 64)
    q2 = b.sb(128, 64)
    negm = b.sb(128, 1)
    elast = b.sb(128, 1)
    och = b.sb(64, DV)
    zz = b.sb(64, 64)

    b.dma(eye[:, :], EYE[:, :])
    b.dma(triu[:, :], TRIU[:, :])
    b.memset(zz[:, :], 0.0)
    b.memset(rmask[:, :], 1.0)
    b.memset(rmask[:, 0:L_:64], 0.0)

    for bb in range(B_):
        t0 = bb * L_
        b.dma(q[:, :], qT[:, t0:t0 + L_])
        b.dma(k_[:, :], kT[:, t0:t0 + L_])
        b.dma(gf[:, :], gfT[:, t0:t0 + L_])
        b.scan(bt[:, :], rmask[:, :], gf[:, :], 0.0, ALU.mult, ALU.add)
        b.act(eb[:, :], bt[:, :], AF.Exp)
        b.memset(S[:, :], 0.0)
        for c in range(64):
            p0 = 64 * c
            ch = slice(p0, p0 + 64)
            b.ts(negm[:, :], bt[:, p0 + 32:p0 + 33], -1.0, ALU.mult)
            b.act(e1[:, :], bt[:, ch], AF.Exp, bias=negm[:, :])
            b.act(e2[:, :], bt[:, ch], AF.Exp, bias=bt[:, p0 + 32:p0 + 33],
                  scale=-1.0)
            b.tt(e1[:, :], e1[:, :], q[:, ch], ALU.mult)
            b.tt(e2[:, :], e2[:, :], k_[:, ch], ALU.mult)
            b.mm(b.ps[0][0:64, 0:64], e2[:, :], e1[:, :])
            b.sel(at[:, :], triu[:, :], b.ps[0][0:64, 0:64], zz[:, :])
            b.tt(q2[:, :], q[:, ch], eb[:, ch], ALU.mult)
            b.dma(vch[:, :], vtok[t0 + p0:t0 + p0 + 64, :])
            b.mm(b.ps[1][0:64, 0:DV], at[:, :], vch[:, :], start=True,
                 stop=False)
            b.mm(b.ps[1][0:64, 0:DV], q2[:, :], S[:, :], start=False,
                 stop=True)
            b.copy(och[:, :], b.ps[1][0:64, 0:DV])
            b.dma(o_o[t0 + p0:t0 + p0 + 64, :], och[:, :])
            b.act(elast[:, :], bt[:, p0 + 63:p0 + 64], AF.Exp)
            b.act(kd[:, :], bt[:, ch], AF.Exp, bias=bt[:, p0 + 63:p0 + 64],
                  scale=-1.0)
            b.tt(kd[:, :], kd[:, :], k_[:, ch], ALU.mult)
            b.trans(b.ps[2][0:64, 0:128], kd[:, :], eye[:, :])
            b.copy(kdt[:, :], b.ps[2][0:64, 0:128])
            b.mm(b.ps[3][:, 0:DV], kdt[:, :], vch[:, :])
            b.ts(S[:, :], S[:, :], elast[:, :], ALU.mult)
            b.tt(S[:, :], S[:, :], b.ps[3][:, 0:DV], ALU.add)
    return b.finalize()


TH = TPC + HALO

def build_LC():
    b = Bld("LC")
    xT = b.inp("xT", [D_, TH])
    xnT = b.inp("xnT", [D_, TH])
    oT = b.inp("oT", [D_, TH])
    gT = b.inp("gT", [D_, TH])
    WogT = b.inp("WogT", [D_, D_])
    WigT = b.inp("WigT", [D_, D_])
    WoT = b.inp("WoT", [D_, D_])
    gnw = b.inp("gnw", [D_, 1])
    Wqkvz = b.inp("Wqkvz", [D_, 12288])
    Wba = b.inp("Wba", [D_, 64])
    convw = b.inp("convw", [FFH, 4])
    convb = b.inp("convb", [FFH, 1])
    alr = b.inp("alr", [64, 1])
    dtb = b.inp("dtb", [64, 1])
    o_x1 = b.outp("x1T", [D_, TPC])
    o_xn1 = b.outp("xn1T", [D_, TPC])
    o_qp = b.outp("qpT", [D_, TPC])
    o_kp = b.outp("kpT", [D_, TPC])
    o_vp = b.outp("vpT", [2 * D_, TPC])
    o_z = b.outp("zT", [2 * D_, TPC])
    o_br = b.outp("betaR", [64, TPC])
    o_gr = b.outp("gR", [64, TPC])

    A16 = [b.sb(128, 256) for _ in range(16)]
    B16 = [b.sb(128, 256) for _ in range(16)]
    C16 = [b.sb(128, 256) for _ in range(16)]
    D16 = [b.sb(128, 256) for _ in range(16)]
    xn1F = [b.sb(128, TH) for _ in range(16)]
    wcA = [b.sb(128, 128) for _ in range(16)]
    wcB = [b.sb(128, 128) for _ in range(16)]
    b._ones = b.sb(128, 128)
    b._scr0 = b.sb(128, 512)
    b._scr1 = b.sb(128, 512)
    b._scr2 = b.sb(128, 512)
    gnt = b.sb(128, 16)
    sgo = b.sb(128, 256)
    ft = b.sb(128, TH)
    acc1 = b.sb(128, TPC)
    acc2 = b.sb(128, TPC)
    cw = b.sb(128, 4)
    cb = b.sb(128, 1)
    alrt = b.sb(64, 1)
    dtbt = b.sb(64, 1)
    ealr = b.sb(64, 1)

    b.memset(b._ones[:, :], 1.0)
    for k in range(16):
        b.dma(gnt[:, k:k + 1], gnw[128 * k:128 * (k + 1), :])
    b.dma(alrt[:, :], alr[:, :])
    b.dma(dtbt[:, :], dtb[:, :])
    b.act(ealr[:, :], alrt[:, :], AF.Exp)

    slabs = [(0, 256), (256, 256), (512, 256), (768, 256), (1024, 3)]
    for t0, n in slabs:
        for k in range(16):
            b.dma(A16[k][:, :n], xT[128 * k:128 * (k + 1), t0:t0 + n])
            b.dma(B16[k][:, :n], oT[128 * k:128 * (k + 1), t0:t0 + n])
            b.dma(C16[k][:, :n], xnT[128 * k:128 * (k + 1), t0:t0 + n])
        part_reduce(b, b.ps[6], [B16[k][:, :n] for k in range(16)], n,
                    sq_tmp=b._scr0)
        b.ts(b._scr1[:, :n], b.ps[6][:, :n], 1.0 / D_, ALU.mult)
        b.act(b._scr1[:, :n], b._scr1[:, :n], AF.Sqrt, bias=b.cbias(EPS0))
        b.op('vector', lambda e, n_=n: e.reciprocal(b._scr1[:, :n_],
                                                    b._scr1[:, :n_]))
        for k in range(16):
            b.tt(D16[k][:, :n], B16[k][:, :n], b._scr1[:, :n], ALU.mult)
            b.ts(D16[k][:, :n], D16[k][:, :n], gnt[:, k:k + 1], ALU.mult)
        for c in range(16):
            for k in range(16):
                b.dma(wcA[k][:, :], WogT[128 * k:128 * (k + 1),
                                         128 * c:128 * (c + 1)])
                b.dma(wcB[k][:, :], WigT[128 * k:128 * (k + 1),
                                         128 * c:128 * (c + 1)])
            proj_f(b, b.ps[0], [wcA[k][:, :] for k in range(16)],
                   [B16[k][:, :n] for k in range(16)], n, stop=False)
            proj_f(b, b.ps[0], [wcB[k][:, :] for k in range(16)],
                   [C16[k][:, :n] for k in range(16)], n, start=False)
            b.act(sgo[:, :n], b.ps[0][:, :n], AF.Silu)
            b.tt(D16[c][:, :n], D16[c][:, :n], sgo[:, :n], ALU.mult)
        for c in range(16):
            for k in range(16):
                b.dma(wcA[k][:, :], WoT[128 * k:128 * (k + 1),
                                        128 * c:128 * (c + 1)])
            proj_f(b, b.ps[0], [wcA[k][:, :] for k in range(16)],
                   [D16[k][:, :n] for k in range(16)], n)
            b.dma(C16[c][:, :n], gT[128 * c:128 * (c + 1), t0:t0 + n])
            b.tt(b._scr0[:, :n], b.ps[0][:, :n], C16[c][:, :n], ALU.mult)
            b.tt(A16[c][:, :n], A16[c][:, :n], b._scr0[:, :n], ALU.add)
            if t0 >= HALO:
                b.dma(o_x1[128 * c:128 * (c + 1), t0 - HALO:t0 - HALO + n],
                      A16[c][:, :n])
        if t0 < HALO:
            for c in range(16):
                b.dma(o_x1[128 * c:128 * (c + 1), 0:t0 + n - HALO],
                      A16[c][:, HALO - t0:n])
        layernorm_f(b, [xn1F[k][:, t0:t0 + n] for k in range(16)],
                    [A16[k][:, :n] for k in range(16)], n)
        for c in range(16):
            if t0 >= HALO:
                b.dma(o_xn1[128 * c:128 * (c + 1), t0 - HALO:t0 - HALO + n],
                      xn1F[c][:, t0:t0 + n])
            else:
                b.dma(o_xn1[128 * c:128 * (c + 1), 0:t0 + n - HALO],
                      xn1F[c][:, HALO:t0 + n])

    pslabs = [(0, 512), (512, 512), (1024, 3)]

    def ftile_proj(fidx):
        for t0, n in pslabs:
            for k in range(16):
                b.dma(wcA[k][:, :], Wqkvz[128 * k:128 * (k + 1),
                                          128 * fidx:128 * (fidx + 1)])
            proj_f(b, b.ps[0], [wcA[k][:, :] for k in range(16)],
                   [xn1F[k][:, t0:t0 + n] for k in range(16)], n)
            b.copy(ft[:, t0:t0 + n], b.ps[0][:, :n])

    def do_conv(crow):
        b.dma(cw[:, :], convw[128 * crow:128 * (crow + 1), :])
        b.dma(cb[:, :], convb[128 * crow:128 * (crow + 1), :])
        b.ts(acc1[:, :], ft[:, 0:TPC], cw[:, 0:1], ALU.mult)
        b.stt(acc2[:, :], ft[:, 1:1 + TPC], cw[:, 1:2], acc1[:, :],
              ALU.mult, ALU.add)
        b.stt(acc1[:, :], ft[:, 2:2 + TPC], cw[:, 2:3], acc2[:, :],
              ALU.mult, ALU.add)
        b.stt(acc2[:, :], ft[:, 3:3 + TPC], cw[:, 3:4], acc1[:, :],
              ALU.mult, ALU.add)
        b.act(acc1[:, :], acc2[:, :], AF.Silu, bias=cb[:, :])

    def l2n(qscale):
        for i, (t0, n) in enumerate([(0, 512), (512, 512)]):
            part_reduce(b, b.ps[1], [acc1[:, t0:t0 + n]], n, sq_tmp=b._scr0)
            b.act(b._scr1[:, :n], b.ps[1][:, :n], AF.Sqrt, bias=b.cbias(EPS1))
            b.op('vector', lambda e, n_=n: e.reciprocal(b._scr1[:, :n_],
                                                        b._scr1[:, :n_]))
            if qscale != 1.0:
                b.ts(b._scr1[:, :n], b._scr1[:, :n], float(qscale), ALU.mult)
            b.tt(acc1[:, t0:t0 + n], acc1[:, t0:t0 + n], b._scr1[:, :n],
                 ALU.mult)

    for j in range(16):
        ftile_proj(6 * j + 0)
        do_conv(j)
        l2n(float(128 ** -0.5))
        b.dma(o_qp[128 * j:128 * (j + 1), :], acc1[:, :])
        ftile_proj(6 * j + 1)
        do_conv(16 + j)
        l2n(1.0)
        b.dma(o_kp[128 * j:128 * (j + 1), :], acc1[:, :])
        for u in range(2):
            ftile_proj(6 * j + 2 + u)
            do_conv(32 + 2 * j + u)
            b.dma(o_vp[128 * (2 * j + u):128 * (2 * j + u + 1), :],
                  acc1[:, :])
        for u in range(2):
            ftile_proj(6 * j + 4 + u)
            b.dma(o_z[128 * (2 * j + u):128 * (2 * j + u + 1), :],
                  ft[:, HALO:])
    for t0, n in pslabs:
        for k in range(16):
            b.dma(wcA[k][:, 0:64], Wba[128 * k:128 * (k + 1), :])
        for k in range(16):
            b.mm(b.ps[0][0:64, :n], wcA[k][:, 0:64],
                 xn1F[k][:, t0:t0 + n], start=(k == 0), stop=(k == 15))
        b.copy(ft[0:64, t0:t0 + n], b.ps[0][0:64, :n])
    b.act(acc1[0:64, :], ft[0:64, HALO:], AF.Sigmoid)
    b.dma(o_br[:, :], acc1[0:64, :])
    b.ts(acc1[0:64, :], ft[0:64, HALO:], dtbt[:, :], ALU.add)
    b.act(acc2[0:64, :], acc1[0:64, :], AF.Abs)
    b.act(acc2[0:64, :], acc2[0:64, :], AF.Exp, scale=-1.0)
    b.act(acc2[0:64, :], acc2[0:64, :], AF.Ln, bias=b.cbias(1.0, p=64))
    b.act(acc1[0:64, :], acc1[0:64, :], AF.Relu)
    b.tt(acc1[0:64, :], acc1[0:64, :], acc2[0:64, :], ALU.add)
    b.ts(acc1[0:64, :], acc1[0:64, :], ealr[:, :], ALU.mult, -1.0, ALU.mult)
    b.dma(o_gr[:, :], acc1[0:64, :])
    return b.finalize()


def build_LD():
    b = Bld("LD")
    CD = 128
    qh = b.inp("qh", [512, L_])
    kh = b.inp("kh", [512, L_])
    vh = b.inp("vh", [1024, L_])
    bet = b.inp("bet", [8, L_])
    gg = b.inp("gg", [8, L_])
    EYE = b.inp("EYE", [128, 128])
    SU = b.inp("SU", [128, 128], mybir.dt.uint8)
    SLo = b.inp("SLo", [128, 128], mybir.dt.uint8)
    TRIU = b.inp("TRIU", [128, 128], mybir.dt.uint8)
    o_o = b.outp("oat", [L_, 1024])

    q = b.sb(128, L_)
    k_ = b.sb(128, L_)
    v = b.sb(128, L_)
    grow = b.sb(1, L_)
    brow = b.sb(1, L_)
    G = b.sb(1, L_)
    rmask = b.sb(1, L_)
    onesc = b.sb(1, 128)
    eye = b.sb(128, 128)
    su = b.sb(128, 128, mybir.dt.uint8)
    slo = b.sb(128, 128, mybir.dt.uint8)
    triu = b.sb(128, 128, mybir.dt.uint8)
    S = b.sb(128, 128)
    e1r = b.sb(1, CD)
    e2r = b.sb(1, CD)
    e3r = b.sb(1, CD)
    e4r = b.sb(1, CD)
    rkb = b.sb(1, CD)
    rka = b.sb(1, CD)
    negm = b.sb(1, 1)
    mcol = b.sb(1, 1)
    KB = b.sb(128, CD)
    KI = b.sb(128, CD)
    KC = b.sb(128, CD)
    KA = b.sb(128, CD)
    Q1 = b.sb(128, CD)
    Q4 = b.sb(128, CD)
    vb = b.sb(128, CD)
    BT = b.sb(128, CD)
    Bl = b.sb(128, CD)
    AoT = b.sb(128, CD)
    Vts = b.sb(128, CD)
    R = b.sb(128, CD)
    Tp = b.sb(128, CD)
    TpT = b.sb(128, CD)
    P1 = b.sb(128, CD)
    P2 = b.sb(128, CD)
    Pt1 = b.sb(128, CD)
    Pt2 = b.sb(128, CD)
    Ds = b.sb(128, CD)
    KCt = b.sb(128, CD)
    Os = b.sb(128, CD)
    aCc = b.sb(128, 1)
    zz = b.sb(128, CD)

    b.dma(eye[:, :], EYE[:, :])
    b.dma(su[:, :], SU[:, :])
    b.dma(slo[:, :], SLo[:, :])
    b.dma(triu[:, :], TRIU[:, :])
    b.memset(onesc[:, :], 1.0)
    b.memset(zz[:, :], 0.0)
    b.memset(rmask[:, :], 1.0)
    b.memset(rmask[:, 0:L_:CD], 0.0)

    for h in range(8):
        kj = h // 2
        b.dma(q[:, :], qh[128 * kj:128 * (kj + 1), :])
        b.dma(k_[:, :], kh[128 * kj:128 * (kj + 1), :])
        b.dma(v[:, :], vh[128 * h:128 * (h + 1), :])
        b.dma(grow[:, :], gg[h:h + 1, :])
        b.dma(brow[:, :], bet[h:h + 1, :])
        b.scan(G[:, :], rmask[:, :], grow[:, :], 0.0, ALU.mult, ALU.add)
        b.memset(S[:, :], 0.0)
        for c in range(L_ // CD):
            p0 = CD * c
            ch = slice(p0, p0 + CD)
            gc = G[0:1, ch]
            b.ts(negm[:, :], G[0:1, p0 + CD - 1:p0 + CD], -0.5, ALU.mult)
            b.ts(mcol[:, :], G[0:1, p0 + CD - 1:p0 + CD], 0.5, ALU.mult)
            b.act(e1r[:, :], gc, AF.Exp, bias=negm[:, :])
            b.act(e2r[:, :], gc, AF.Exp, bias=mcol[:, :], scale=-1.0)
            b.act(e3r[:, :], gc, AF.Exp,
                  bias=G[0:1, p0 + CD - 1:p0 + CD], scale=-1.0)
            b.act(e4r[:, :], gc, AF.Exp)
            b.tt(rkb[:, :], e1r[:, :], brow[0:1, ch], ALU.mult)
            b.tt(rka[:, :], e4r[:, :], brow[0:1, ch], ALU.mult)
            for row, dst, srctile in ((rkb, KB, k_), (e2r, KI, k_),
                                      (e3r, KC, k_), (rka, KA, k_),
                                      (e1r, Q1, q), (e4r, Q4, q)):
                b.mm(b.ps[5][:, :CD], onesc[:, :], row[:, :])
                b.tt(dst[:, :], srctile[:, ch], b.ps[5][:, :CD], ALU.mult)
            b.mm(b.ps[5][:, :CD], onesc[:, :], brow[0:1, ch])
            b.copy(aCc[:, :], b.ps[5][:, 0:1])
            b.tt(vb[:, :], v[:, ch], b.ps[5][:, :CD], ALU.mult)
            b.mm(b.ps[4][:, :CD], onesc[:, :], e4r[:, :])
            b.copy(aCc[:, :], b.ps[4][:, CD - 1:CD])
            b.mm(b.ps[0][:, :CD], KI[:, :], KB[:, :])
            b.sel(BT[:, :], su[:, :], b.ps[0][:, :CD], zz[:, :])
            b.mm(b.ps[0][:, :CD], KB[:, :], KI[:, :])
            b.sel(Bl[:, :], slo[:, :], b.ps[0][:, :CD], zz[:, :])
            b.mm(b.ps[0][:, :CD], KI[:, :], Q1[:, :])
            b.sel(AoT[:, :], triu[:, :], b.ps[0][:, :CD], zz[:, :])
            b.mm(b.ps[0][:, :CD], KA[:, :], S[:, :])
            b.trans(b.ps[1][:, :CD], vb[:, :], eye[:, :])
            b.copy(Vts[:, :], b.ps[1][:, :CD])
            b.tt(R[:, :], Vts[:, :], b.ps[0][:, :CD], ALU.subtract)
            b.tt(Tp[:, :], eye[:, :], BT[:, :], ALU.subtract)
            b.tt(TpT[:, :], eye[:, :], Bl[:, :], ALU.subtract)
            Pc, Ptc, Pn, Ptn = P1, Pt1, P2, Pt2
            b.copy(Pc[:, :], BT[:, :])
            b.copy(Ptc[:, :], Bl[:, :])
            for s in range(1, 7):
                b.mm(b.ps[0][:, :CD], Ptc[:, :], Pc[:, :])
                b.copy(Pn[:, :], b.ps[0][:, :CD])
                if s < 6:
                    b.mm(b.ps[1][:, :CD], Pc[:, :], Ptc[:, :])
                    b.copy(Ptn[:, :], b.ps[1][:, :CD])
                b.mm(b.ps[2][:, :CD], TpT[:, :], Pn[:, :])
                b.mm(b.ps[3][:, :CD], Pn[:, :], TpT[:, :])
                b.tt(Tp[:, :], Tp[:, :], b.ps[2][:, :CD], ALU.subtract)
                b.tt(TpT[:, :], TpT[:, :], b.ps[3][:, :CD], ALU.subtract)
                Pc, Pn = Pn, Pc
                Ptc, Ptn = Ptn, Ptc
            b.mm(b.ps[0][:, :CD], Tp[:, :], R[:, :])
            b.copy(Ds[:, :], b.ps[0][:, :CD])
            b.mm(b.ps[1][:, :CD], AoT[:, :], Ds[:, :], start=True, stop=False)
            b.mm(b.ps[1][:, :CD], Q4[:, :], S[:, :], start=False, stop=True)
            b.copy(Os[:, :], b.ps[1][:, :CD])
            b.dma(o_o[p0:p0 + CD, 128 * h:128 * (h + 1)], Os[:, :])
            b.trans(b.ps[2][:, :CD], KC[:, :], eye[:, :])
            b.copy(KCt[:, :], b.ps[2][:, :CD])
            b.mm(b.ps[3][:, :CD], KCt[:, :], Ds[:, :])
            b.ts(S[:, :], S[:, :], aCc[:, :], ALU.mult)
            b.tt(S[:, :], S[:, :], b.ps[3][:, :CD], ALU.add)
    return b.finalize()


def build_LE():
    b = Bld("LE")
    oaT = b.inp("oaT", [2 * D_, TPC])
    zT = b.inp("zT", [2 * D_, TPC])
    xn1T = b.inp("xn1T", [D_, TPC])
    Wo1 = b.inp("Wo1", [2 * D_, D_])
    nw = b.inp("nw", [128, 1])
    WrT = b.inp("WrT", [D_, 3])
    WdT = b.inp("WdT", [D_, 32])
    WuT = b.inp("WuT", [32, D_])
    sbias = b.inp("sbias", [D_, 1])
    l1s = b.inp("l1s", [D_, 1])
    l1b = b.inp("l1b", [D_, 1])
    WqT1 = b.inp("WqT1", [D_, D_])
    WkT1 = b.inp("WkT1", [D_, D_])
    WvT1 = b.inp("WvT1", [D_, D_])
    o_s1 = b.outp("s1T", [D_, TPC])
    o_m0 = b.outp("m0", [1, TPC])
    o_m1 = b.outp("m1", [1, TPC])
    o_o0 = b.outp("o0T", [D_, TPC])
    o_o1 = b.outp("o1T", [D_, TPC])
    o_q = b.outp("qT1", [D_, TPC])
    o_k = b.outp("kT1", [D_, TPC])
    o_v = b.outp("vT1", [D_, TPC])

    A32 = [b.sb(128, 512) for _ in range(32)]
    X16 = [b.sb(128, 512) for _ in range(16)]
    N16 = [b.sb(128, 512) for _ in range(16)]
    wc = [b.sb(128, 128) for _ in range(16)]
    b._ones = b.sb(128, 128)
    b._scr0 = b.sb(128, 512)
    b._scr1 = b.sb(128, 512)
    b._scr2 = b.sb(128, 512)
    nwt = b.sb(128, 1)
    cvec = [b.sb(128, 16) for _ in range(4)]
    lr = b.sb(32, 512)
    l0r = b.sb(1, 512)
    l1r = b.sb(1, 512)
    l2r = b.sb(1, 512)
    m0r = b.sb(1, 512)
    m1r = b.sb(1, 512)

    b.memset(b._ones[:, :], 1.0)
    b.dma(nwt[:, :], nw[:, :])
    for k in range(16):
        b.dma(cvec[0][:, k:k + 1], sbias[128 * k:128 * (k + 1), :])
        b.dma(cvec[1][:, k:k + 1], l1s[128 * k:128 * (k + 1), :])
        b.dma(cvec[2][:, k:k + 1], l1b[128 * k:128 * (k + 1), :])

    for sl in range(2):
        t0 = 512 * sl
        n = 512
        for k in range(16):
            b.dma(X16[k][:, :n], xn1T[128 * k:128 * (k + 1), t0:t0 + n])
        for k in range(32):
            b.dma(A32[k][:, :n], oaT[128 * k:128 * (k + 1), t0:t0 + n])
        for k in range(32):
            b.dma(b._scr0[:, :n], zT[128 * k:128 * (k + 1), t0:t0 + n])
            b.act(b._scr0[:, :n], b._scr0[:, :n], AF.Silu)
            b.tt(A32[k][:, :n], A32[k][:, :n], b._scr0[:, :n], ALU.mult)
            part_reduce(b, b.ps[6], [A32[k][:, :n]], n, sq_tmp=b._scr0)
            b.ts(b._scr1[:, :n], b.ps[6][:, :n], 1.0 / 128.0, ALU.mult)
            b.act(b._scr1[:, :n], b._scr1[:, :n], AF.Sqrt, bias=b.cbias(EPS1))
            b.op('vector', lambda e, n_=n: e.reciprocal(b._scr1[:, :n_],
                                                        b._scr1[:, :n_]))
            b.tt(A32[k][:, :n], A32[k][:, :n], b._scr1[:, :n], ALU.mult)
            b.ts(A32[k][:, :n], A32[k][:, :n], nwt[:, :], ALU.mult)
        for c in range(16):
            for k in range(32):
                if k % 16 == 0:
                    for kk in range(16):
                        b.dma(wc[kk][:, :],
                              Wo1[128 * (k + kk):128 * (k + kk + 1),
                                  128 * c:128 * (c + 1)])
                b.mm(b.ps[0][:, :n], wc[k % 16][:, :], A32[k][:, :n],
                     start=(k == 0), stop=(k == 31))
            b.copy(b._scr0[:, :n], b.ps[0][:, :n])
            b.dma(o_s1[128 * c:128 * (c + 1), t0:t0 + n], b._scr0[:, :n])
        for r in range(3):
            for k in range(16):
                b.dma(wc[k][:, 0:3], WrT[128 * k:128 * (k + 1), :])
                b.mm(b.ps[1][0:1, :n], wc[k][:, r:r + 1], X16[k][:, :n],
                     start=(k == 0), stop=(k == 15))
            b.copy([l0r, l1r, l2r][r][:, :n], b.ps[1][0:1, :n])
        b.tt(m0r[:, :n], l0r[:, :n], l1r[:, :n], ALU.is_ge)
        b.tt(b._scr0[0:1, :n], l0r[:, :n], l2r[:, :n], ALU.is_ge)
        b.tt(m0r[:, :n], m0r[:, :n], b._scr0[0:1, :n], ALU.mult)
        b.tt(m1r[:, :n], l1r[:, :n], l0r[:, :n], ALU.is_gt)
        b.tt(b._scr0[0:1, :n], l1r[:, :n], l2r[:, :n], ALU.is_ge)
        b.tt(m1r[:, :n], m1r[:, :n], b._scr0[0:1, :n], ALU.mult)
        b.dma(o_m0[:, t0:t0 + n], m0r[:, :n])
        b.dma(o_m1[:, t0:t0 + n], m1r[:, :n])
        for k in range(16):
            b.ts(b._scr0[:, :n], X16[k][:, :n], cvec[0][:, k:k + 1], ALU.add)
            b.dma(o_o0[128 * k:128 * (k + 1), t0:t0 + n], b._scr0[:, :n])
        for k in range(16):
            b.dma(wc[k][:, 0:32], WdT[128 * k:128 * (k + 1), :])
            b.mm(b.ps[2][0:32, :n], wc[k][:, 0:32], X16[k][:, :n],
                 start=(k == 0), stop=(k == 15))
        b.copy(lr[:, :n], b.ps[2][0:32, :n])
        for c in range(16):
            b.dma(wc[0][0:32, :], WuT[:, 128 * c:128 * (c + 1)])
            b.mm(b.ps[2][:, :n], wc[0][0:32, :], lr[:, :n])
            b.tt(b._scr0[:, :n], b.ps[2][:, :n], X16[c][:, :n], ALU.add)
            b.dma(o_o1[128 * c:128 * (c + 1), t0:t0 + n], b._scr0[:, :n])
        layernorm_f(b, [N16[k][:, :n] for k in range(16)],
                    [X16[k][:, :n] for k in range(16)], n,
                    scale_tiles=[cvec[1][:, k:k + 1] for k in range(16)],
                    bias_tiles=[cvec[2][:, k:k + 1] for k in range(16)])
        for wmat, dst in ((WqT1, o_q), (WkT1, o_k), (WvT1, o_v)):
            for c in range(16):
                for k in range(16):
                    b.dma(wc[k][:, :], wmat[128 * k:128 * (k + 1),
                                            128 * c:128 * (c + 1)])
                proj_f(b, b.ps[0], [wc[k][:, :] for k in range(16)],
                       [N16[k][:, :n] for k in range(16)], n)
                b.copy(b._scr0[:, :n], b.ps[0][:, :n])
                b.dma(dst[128 * c:128 * (c + 1), t0:t0 + n], b._scr0[:, :n])
    return b.finalize()


def build_LF():
    b = Bld("LF")
    qT = b.inp("qTf", [512, L_])
    kT = b.inp("kTf", [512, L_])
    vT = b.inp("vTf", [512, L_])
    EYE = b.inp("EYE", [128, 128])
    MSKA = b.inp("MSKA", [512, 512])
    o_c = b.outp("ctxT", [512, L_])

    q = b.sb(128, L_)
    k_ = b.sb(128, L_)
    v = b.sb(128, L_)
    vtok = [b.sb(128, 128) for _ in range(32)]
    eye = b.sb(128, 128)
    mska = [b.sb(128, 512) for _ in range(4)]
    pt = b.sb(128, 512)
    rl = b.sb(128, 512)
    cx = b.sb(128, 512)
    b._ones = b.sb(128, 128)

    b.memset(b._ones[:, :], 1.0)
    b.dma(eye[:, :], EYE[:, :])
    for s in range(4):
        b.dma(mska[s][:, :], MSKA[128 * s:128 * (s + 1), :])

    sc = float(1.0 / np.sqrt(128.0))
    for h in range(4):
        b.dma(q[:, :], qT[128 * h:128 * (h + 1), :])
        b.dma(k_[:, :], kT[128 * h:128 * (h + 1), :])
        b.dma(v[:, :], vT[128 * h:128 * (h + 1), :])
        for t in range(32):
            b.trans(b.ps[0][:, :128], v[:, 128 * t:128 * (t + 1)], eye[:, :])
            b.copy(vtok[t][:, :], b.ps[0][:, :128])
        for B in range(8):
            tq = slice(512 * B, 512 * (B + 1))
            nk = 4 * B + 4
            for kb in range(nk):
                b.mm(b.ps[0][:, :512], k_[:, 128 * kb:128 * (kb + 1)],
                     q[:, tq])
                b.act(pt[:, :], b.ps[0][:, :512], AF.Exp, scale=sc)
                s = kb - 4 * B
                if s >= 0:
                    b.tt(pt[:, :], pt[:, :], mska[s][:, :], ALU.mult)
                b.mm(b.ps[1][:, :512], b._ones[:, :], pt[:, :],
                     start=(kb == 0), stop=(kb == nk - 1))
                b.mm(b.ps[2][:, :512], vtok[kb][:, :], pt[:, :],
                     start=(kb == 0), stop=(kb == nk - 1))
            b.op('vector', lambda e: e.reciprocal(rl[:, :], b.ps[1][:, :512]))
            b.tt(cx[:, :], b.ps[2][:, :512], rl[:, :], ALU.mult)
            b.dma(o_c[128 * h:128 * (h + 1), tq], cx[:, :])
    return b.finalize()


def build_LG():
    b = Bld("LG")
    ctxT = b.inp("ctxT", [D_, TPC])
    xn1T = b.inp("xn1T", [D_, TPC])
    x1T = b.inp("x1T", [D_, TPC])
    s1T = b.inp("s1T", [D_, TPC])
    o0T = b.inp("o0T", [D_, TPC])
    o1T = b.inp("o1T", [D_, TPC])
    m0 = b.inp("m0", [1, TPC])
    m1 = b.inp("m1", [1, TPC])
    WoT2 = b.inp("WoT2", [D_, D_])
    l2s = b.inp("l2s", [D_, 1])
    l2b = b.inp("l2b", [D_, 1])
    Wf1T = b.inp("Wf1T", [D_, FFH])
    Wf2T = b.inp("Wf2T", [FFH, D_])
    o_y = b.outp("yT", [D_, TPC])

    X2 = [b.sb(128, 512) for _ in range(16)]
    XB = [b.sb(128, 512) for _ in range(16)]
    CT = [b.sb(128, 512) for _ in range(16)]
    FF = [b.sb(128, 512) for _ in range(16)]
    wc = [b.sb(128, 128) for _ in range(16)]
    b._ones = b.sb(128, 128)
    b._scr0 = b.sb(128, 512)
    b._scr1 = b.sb(128, 512)
    b._scr2 = b.sb(128, 512)
    cvec = [b.sb(128, 16) for _ in range(2)]
    hidt = b.sb(128, 512)
    m0b = b.sb(128, 512)
    m1b = b.sb(128, 512)
    m0r = b.sb(1, 512)
    m1r = b.sb(1, 512)
    onesr = b.sb(1, 128)

    b.memset(b._ones[:, :], 1.0)
    b.memset(onesr[:, :], 1.0)
    for k in range(16):
        b.dma(cvec[0][:, k:k + 1], l2s[128 * k:128 * (k + 1), :])
        b.dma(cvec[1][:, k:k + 1], l2b[128 * k:128 * (k + 1), :])

    GA = 0.044715
    GB = float(np.sqrt(2.0 / np.pi))
    for sl in range(2):
        t0 = 512 * sl
        n = 512
        for k in range(16):
            b.dma(CT[k][:, :n], ctxT[128 * k:128 * (k + 1), t0:t0 + n])
            b.dma(X2[k][:, :n], xn1T[128 * k:128 * (k + 1), t0:t0 + n])
        for c in range(16):
            for k in range(16):
                b.dma(wc[k][:, :], WoT2[128 * k:128 * (k + 1),
                                        128 * c:128 * (c + 1)])
            proj_f(b, b.ps[0], [wc[k][:, :] for k in range(16)],
                   [CT[k][:, :n] for k in range(16)], n)
            b.tt(X2[c][:, :n], X2[c][:, :n], b.ps[0][:, :n], ALU.add)
        layernorm_f(b, [XB[k][:, :n] for k in range(16)],
                    [X2[k][:, :n] for k in range(16)], n,
                    scale_tiles=[cvec[0][:, k:k + 1] for k in range(16)],
                    bias_tiles=[cvec[1][:, k:k + 1] for k in range(16)])
        for cg, (c0, cn) in enumerate(((0, 6), (6, 6), (12, 4))):
            for ht in range(64):
                for k in range(16):
                    b.dma(wc[k][:, :], Wf1T[128 * k:128 * (k + 1),
                                            128 * ht:128 * (ht + 1)])
                proj_f(b, b.ps[6], [wc[k][:, :] for k in range(16)],
                       [XB[k][:, :n] for k in range(16)], n)
                b.copy(hidt[:, :n], b.ps[6][:, :n])
                b.act(b._scr0[:, :n], hidt[:, :n], AF.Square)
                b.tt(b._scr0[:, :n], b._scr0[:, :n], hidt[:, :n], ALU.mult)
                b.stt(b._scr0[:, :n], b._scr0[:, :n], GA, hidt[:, :n],
                      ALU.mult, ALU.add)
                b.act(b._scr0[:, :n], b._scr0[:, :n], AF.Tanh, scale=GB)
                b.ts(b._scr0[:, :n], b._scr0[:, :n], 1.0, ALU.add)
                b.tt(b._scr0[:, :n], b._scr0[:, :n], hidt[:, :n], ALU.mult)
                b.ts(hidt[:, :n], b._scr0[:, :n], 0.5, ALU.mult)
                for ci in range(cn):
                    c = c0 + ci
                    b.dma(wc[0][:, :], Wf2T[128 * ht:128 * (ht + 1),
                                            128 * c:128 * (c + 1)])
                    b.mm(b.ps[ci][:, :n], wc[0][:, :], hidt[:, :n],
                         start=(ht == 0), stop=(ht == 63))
            for ci in range(cn):
                b.copy(FF[c0 + ci][:, :n], b.ps[ci][:, :n])
        b.dma(m0r[:, :n], m0[:, t0:t0 + n])
        b.dma(m1r[:, :n], m1[:, t0:t0 + n])
        b.mm(b.ps[0][:, :n], onesr[:, :], m0r[:, :n])
        b.copy(m0b[:, :n], b.ps[0][:, :n])
        b.mm(b.ps[0][:, :n], onesr[:, :], m1r[:, :n])
        b.copy(m1b[:, :n], b.ps[0][:, :n])
        for c in range(16):
            b.tt(FF[c][:, :n], FF[c][:, :n], X2[c][:, :n], ALU.add)
            b.dma(b._scr0[:, :n], o0T[128 * c:128 * (c + 1), t0:t0 + n])
            b.tt(b._scr0[:, :n], b._scr0[:, :n], FF[c][:, :n], ALU.subtract)
            b.tt(b._scr0[:, :n], b._scr0[:, :n], m0b[:, :n], ALU.mult)
            b.tt(FF[c][:, :n], FF[c][:, :n], b._scr0[:, :n], ALU.add)
            b.dma(b._scr0[:, :n], o1T[128 * c:128 * (c + 1), t0:t0 + n])
            b.tt(b._scr0[:, :n], b._scr0[:, :n], FF[c][:, :n], ALU.subtract)
            b.tt(b._scr0[:, :n], b._scr0[:, :n], m1b[:, :n], ALU.mult)
            b.tt(FF[c][:, :n], FF[c][:, :n], b._scr0[:, :n], ALU.add)
            b.act(FF[c][:, :n], FF[c][:, :n], AF.Silu)
            b.dma(b._scr1[:, :n], s1T[128 * c:128 * (c + 1), t0:t0 + n])
            b.tt(FF[c][:, :n], FF[c][:, :n], b._scr1[:, :n], ALU.mult)
            b.dma(b._scr1[:, :n], x1T[128 * c:128 * (c + 1), t0:t0 + n])
            b.tt(FF[c][:, :n], FF[c][:, :n], b._scr1[:, :n], ALU.add)
            b.dma(o_y[128 * c:128 * (c + 1), t0:t0 + n], FF[c][:, :n])
    return b.finalize()


def _T(a):
    return np.ascontiguousarray(np.asarray(a, dtype=np.float32).T)


def _C(a):
    return np.ascontiguousarray(np.asarray(a, dtype=np.float32))


def _consts():
    H = np.array([[1.0]], dtype=np.float32)
    while H.shape[0] < 128:
        H = np.block([[H, H], [H, -H]]).astype(np.float32)
    EYE = np.eye(128, dtype=np.float32)
    TRIU64 = np.triu(np.ones((64, 64), np.float32))
    SU128 = np.triu(np.ones((128, 128), np.float32), k=1)
    SL128 = np.tril(np.ones((128, 128), np.float32), k=-1)
    TRIU128 = np.triu(np.ones((128, 128), np.float32))
    mska = np.zeros((512, 512), np.float32)
    for s in range(4):
        i = np.arange(128)[:, None]
        j = np.arange(512)[None, :]
        mska[128 * s:128 * (s + 1)] = (j >= i + 128 * s).astype(np.float32)
    return H, EYE, TRIU64, SU128, SL128, TRIU128, mska


_BUILT = {}


def _get(name, fn):
    if name not in _BUILT:
        _BUILT[name] = fn()
    return _BUILT[name]


def _halo_ext(M, c):
    s = TPC * c
    if c % 4 == 0:
        z = np.zeros((M.shape[0], HALO), np.float32)
        return np.concatenate([z, M[:, s:s + TPC]], axis=1)
    return _C(M[:, s - HALO:s + TPC])


def kernel(**inp):
    H128, EYE, TRIU64, SU128, SL128, TRIU128, MSKA = _consts()
    g = lambda n: np.asarray(inp[n], dtype=np.float32)

    x = g("L0_attn_hidden_states").reshape(NTOK, D_)
    xT = _T(x)

    ncA = _get("LA", build_LA)
    WqT = _T(g("L0_attn_Wq"))
    WkT = _T(g("L0_attn_Wk"))
    W1T = _T(g("L0_ffn_W1"))
    W3T = _T(g("L0_ffn_W3"))
    W2T = _T(g("L0_ffn_W2"))
    maps = [dict(xT=_C(xT[:, TPC * c:TPC * (c + 1)]), WqT=WqT, WkT=WkT,
                 W1T=W1T, W3T=W3T, W2T=W2T, H128=H128) for c in range(NCORE)]
    rs = _run(ncA, maps)
    xnT = np.hstack([r["xnT"] for r in rs])
    qT = np.hstack([r["qT"] for r in rs])
    kT = np.hstack([r["kT"] for r in rs])
    gfT = np.hstack([r["gfT"] for r in rs])
    gT = np.hstack([r["gT"] for r in rs])

    ncB = _get("LB", build_LB)
    xn_tok = _T(xnT)
    maps = [dict(qT=qT, kT=kT, gfT=gfT,
                 vtok=_C(xn_tok[:, 256 * c:256 * (c + 1)]),
                 EYE=EYE, TRIU=TRIU64.astype(np.uint8)) for c in range(NCORE)]
    rs = _run(ncB, maps)
    o_tok = np.hstack([r["o_o"] for r in rs])
    oT = _T(o_tok)

    ncC = _get("LC", build_LC)
    alr = np.zeros((64, 1), np.float32)
    dtb = np.zeros((64, 1), np.float32)
    al = g("L1_attn_a_log")
    dt = g("L1_attn_dt_bias")
    for j in range(16):
        for gg_ in range(2):
            alr[4 * j + 2 + gg_, 0] = al[2 * j + gg_]
            dtb[4 * j + 2 + gg_, 0] = dt[2 * j + gg_]
    maps = [dict(xT=_halo_ext(xT, c), xnT=_halo_ext(xnT, c),
                 oT=_halo_ext(oT, c), gT=_halo_ext(gT, c),
                 WogT=_T(g("L0_attn_Wog")), WigT=_T(g("L0_attn_Wig")),
                 WoT=_T(g("L0_attn_Wo")),
                 gnw=_C(g("L0_attn_g_norm_w").reshape(-1, 1)),
                 Wqkvz=g("L1_attn_w_qkvz"), Wba=g("L1_attn_w_ba"),
                 convw=g("L1_attn_conv_w"),
                 convb=_C(g("L1_attn_conv_b").reshape(-1, 1)),
                 alr=alr, dtb=dtb) for c in range(NCORE)]
    rs = _run(ncC, maps)
    x1T = np.hstack([r["x1T"] for r in rs])
    xn1T = np.hstack([r["xn1T"] for r in rs])
    qpT = np.hstack([r["qpT"] for r in rs])
    kpT = np.hstack([r["kpT"] for r in rs])
    vpT = np.hstack([r["vpT"] for r in rs])
    zT = np.hstack([r["zT"] for r in rs])
    betaR = np.hstack([r["betaR"] for r in rs])
    gR = np.hstack([r["gR"] for r in rs])
    beta32 = np.zeros((32, NTOK), np.float32)
    g32 = np.zeros((32, NTOK), np.float32)
    for j in range(16):
        for gg_ in range(2):
            beta32[2 * j + gg_] = betaR[4 * j + gg_]
            g32[2 * j + gg_] = gR[4 * j + 2 + gg_]

    ncD = _get("LD", build_LD)
    maps = []
    for c in range(NCORE):
        bq, kq = c // 4, c % 4
        ts = slice(L_ * bq, L_ * (bq + 1))
        maps.append(dict(qh=_C(qpT[512 * kq:512 * (kq + 1), ts]),
                         kh=_C(kpT[512 * kq:512 * (kq + 1), ts]),
                         vh=_C(vpT[1024 * kq:1024 * (kq + 1), ts]),
                         bet=_C(beta32[8 * kq:8 * (kq + 1), ts]),
                         gg=_C(g32[8 * kq:8 * (kq + 1), ts]),
                         EYE=EYE, SU=SU128.astype(np.uint8), SLo=SL128.astype(np.uint8), TRIU=TRIU128.astype(np.uint8)))
    rs = _run(ncD, maps)
    O_tok = np.zeros((NTOK, 2 * D_), np.float32)
    for c in range(NCORE):
        bq, kq = c // 4, c % 4
        O_tok[L_ * bq:L_ * (bq + 1), 1024 * kq:1024 * (kq + 1)] = rs[c]["oat"]
    oaT = _T(O_tok)

    ncE = _get("LE", build_LE)
    sl_ = lambda M, c: _C(M[:, TPC * c:TPC * (c + 1)])
    maps = [dict(oaT=sl_(oaT, c), zT=sl_(zT, c), xn1T=sl_(xn1T, c),
                 Wo1=g("L1_attn_w_o"),
                 nw=_C(g("L1_attn_norm_w").reshape(-1, 1)),
                 WrT=_T(g("L1_ffn_w_router")), WdT=_T(g("L1_ffn_w_down")),
                 WuT=_T(g("L1_ffn_w_up")),
                 sbias=_C(g("L1_ffn_simple_bias").reshape(-1, 1)),
                 l1s=_C(g("L1_ffn_ln1_scale").reshape(-1, 1)),
                 l1b=_C(g("L1_ffn_ln1_bias").reshape(-1, 1)),
                 WqT1=_T(g("L1_ffn_wq")), WkT1=_T(g("L1_ffn_wk")),
                 WvT1=_T(g("L1_ffn_wv"))) for c in range(NCORE)]
    rs = _run(ncE, maps)
    s1T = [r["s1T"] for r in rs]
    m0s = [r["m0"] for r in rs]
    m1s = [r["m1"] for r in rs]
    o0s = [r["o0T"] for r in rs]
    o1s = [r["o1T"] for r in rs]
    qT1 = np.hstack([r["qT1"] for r in rs])
    kT1 = np.hstack([r["kT1"] for r in rs])
    vT1 = np.hstack([r["vT1"] for r in rs])

    ncF = _get("LF", build_LF)
    maps = []
    for c in range(NCORE):
        bq, hq = c // 4, c % 4
        ts = slice(L_ * bq, L_ * (bq + 1))
        maps.append(dict(qTf=_C(qT1[512 * hq:512 * (hq + 1), ts]),
                         kTf=_C(kT1[512 * hq:512 * (hq + 1), ts]),
                         vTf=_C(vT1[512 * hq:512 * (hq + 1), ts]),
                         EYE=EYE, MSKA=MSKA))
    rs = _run(ncF, maps)
    ctx_full = np.zeros((D_, NTOK), np.float32)
    for c in range(NCORE):
        bq, hq = c // 4, c % 4
        ctx_full[512 * hq:512 * (hq + 1),
                 L_ * bq:L_ * (bq + 1)] = rs[c]["ctxT"]

    ncG = _get("LG", build_LG)
    maps = [dict(ctxT=sl_(ctx_full, c), xn1T=sl_(xn1T, c), x1T=sl_(x1T, c),
                 s1T=s1T[c], o0T=o0s[c], o1T=o1s[c], m0=m0s[c], m1=m1s[c],
                 WoT2=_T(g("L1_ffn_wo")),
                 l2s=_C(g("L1_ffn_ln2_scale").reshape(-1, 1)),
                 l2b=_C(g("L1_ffn_ln2_bias").reshape(-1, 1)),
                 Wf1T=_T(g("L1_ffn_w_ff1")), Wf2T=_T(g("L1_ffn_w_ff2")))
            for c in range(NCORE)]
    rs = _run(ncG, maps)
    yT = np.hstack([r["yT"] for r in rs])
    return _T(yT).reshape(B_, L_, D_)
